# version 07 (final)
# baseline (speedup 1.0000x reference)
import numpy as np
import ml_dtypes
import concourse.bass as bass
import concourse.mybir as mybir
from concourse.bass_utils import run_bass_kernel_spmd

F32 = mybir.dt.float32
BF16 = mybir.dt.bfloat16
I32 = mybir.dt.int32
AF = mybir.ActivationFunctionType
ALU = mybir.AluOpType
AX = mybir.AxisListType
BF = ml_dtypes.bfloat16

D = 1024
FF = 4096
NH = 16
HD = 64
SEG = 1024
NEV = 512
NSEG = 2
R = 16
CLIP = 126.99998
EPS = 1e-5


class Sem:
    def __init__(self, nc, name):
        self._cm = nc.semaphore(name)
        self.h = self._cm.__enter__()
        self.n = 0


class Prog:
    ENGS = ("sync", "tensor", "vector", "scalar", "gpsimd")

    def __init__(self):
        self.q = {e: [] for e in self.ENGS}
        self.lastwait = {e: {} for e in self.ENGS}

    def op(self, eng, fn, inc=None, k=1):
        n = None
        if inc is not None:
            inc.n += k
            n = inc.n
        def go(e, fn=fn, inc=inc, k=k):
            ins = fn(e)
            if inc is not None:
                ins.then_inc(inc.h, k)
        self.q[eng].append(go)
        return n

    def wait(self, eng, sem, target=None):
        t = sem.n if target is None else target
        if t is None or t <= 0:
            return
        if self.lastwait[eng].get(id(sem), -1) >= t:
            return
        self.lastwait[eng][id(sem)] = t
        self.q[eng].append(lambda e, s=sem, t=t: e.wait_ge(s.h, t))


def build(dumps=(), stage=9, dbgseg=0):
    nc = bass.Bass("TRN2", target_bir_lowering=False,
                   detect_race_conditions=False,
                   dynamic_dma_scratch_size=4096)

    def din(name, shape):
        return nc.dram_tensor(name, shape, F32, kind="ExternalInput").ap()

    xev = din("xev", [NSEG, NEV, D])
    xod = din("xod", [NSEG, NEV, D])
    c1 = din("c1", [1, D])
    wmod = din("wmod", [D, 6 * D])
    wfcT = din("wfcT", [D, FF])
    bfc_c = din("bfc_c", [128, FF // 128])
    afcT = din("afcT", [D, R])
    b2fcT = din("b2fcT", [R, FF])
    wpjT = din("wpjT", [FF, D])
    bpj_c = din("bpj_c", [128, D // 128])
    apjT = din("apjT", [FF, R])
    b2pjT = din("b2pjT", [R, D])
    yev = nc.dram_tensor("yev", [NSEG, NEV, D], F32, kind="ExternalOutput").ap()
    yod = nc.dram_tensor("yod", [NSEG, NEV, D], F32, kind="ExternalOutput").ap()

    A = nc.alloc_sbuf_tensor

    identf = A("identf", [128, 128], F32)
    identb = A("identb", [128, 128], BF16)
    onescolf = A("onescolf", [1, 128], F32)
    onescolb = A("onescolb", [1, 128], BF16)

    WFC = [A(f"wfcT{i}", [128, FF], BF16) for i in range(8)]
    WPR = A("wprR", [128, 16384], F32)
    WPJ = [WPR.ap().bitcast(BF16)[:, 1024 * i:1024 * (i + 1)] for i in range(32)]
    wmodbuf = [WPR.ap().bitcast(BF16)[:, 6144:9216],
               WPR.ap().bitcast(BF16)[:, 9216:12288]]
    modrow = WPR.ap()[0:1, 6144:6144 + 6144]
    modrow_b = WPR.ap().bitcast(BF16)[0:1, 0:6144]

    XEV = [A(f"xevt{i}", [128, D], F32) for i in range(4)]
    QSC = A("qscr", [128, 1536], F32)
    qa_s = QSC.ap()[:, 0:512]
    ti_s = QSC.ap().bitcast(I32)[:, 512:1024]
    sg_s = QSC.ap()[:, 1024:1536]
    afc_stage = QSC.ap()[:, 0:128]
    apj_stage = QSC.ap()[:, 128:640]

    HEV = A("heven", [128, 2048], F32)
    heven = [HEV.ap().bitcast(BF16)[:, 1024 * i:1024 * (i + 1)] for i in range(4)]
    iotat = HEV.ap()[:, 0:128]
    XOD = [HEV.ap()[:, 1024 * i:1024 * (i + 1)] for i in range(2)]
    QT = A("qTreg", [128, 512], F32)
    qT = [QT.ap().bitcast(BF16)[0:64, 512 * i:512 * (i + 1)] for i in range(2)]
    ER = A("Ereg", [128, 2048], F32)
    Et = [[ER.ap().bitcast(BF16)[:, (b * 4 + i) * 512:(b * 4 + i + 1) * 512]
           for i in range(4)] for b in range(2)]
    RA = A("R_A", [128, 2048], F32)
    aos = [RA.ap().bitcast(BF16)[:, 1024 * i:1024 * (i + 1)] for i in range(4)]
    h2tm = [RA.ap().bitcast(BF16)[:, 1024 * i:1024 * (i + 1)] for i in range(2)]
    mscr = RA.ap().bitcast(BF16)[:, 2048:3072]
    q2scr = RA.ap().bitcast(BF16)[:, 3072:4096]
    PSB = A("pairsbT", [128, 512], BF16)
    pairsb = PSB.ap()
    RB = A("R_B", [128, 2048], F32)
    h2T = [RB.ap().bitcast(BF16)[:, 256 * i:256 * (i + 1)] for i in range(8)]
    t2f = RB.ap()[:, 1024:2048]
    csil = RB.ap()[0:1, 1024:2048]
    HR = A("h1reg", [128, 4096], F32)
    h1 = [HR.ap().bitcast(BF16)[:, 256 * i:256 * (i + 1)] for i in range(32)]
    qbuf = [HR.ap()[:, 0:2048], HR.ap()[:, 2048:4096]]
    ffr = [RB.ap().bitcast(BF16)[:, 2048:2304], RB.ap().bitcast(BF16)[:, 2304:2560]]

    AFCT = A("afct", [128, 16 * 8], BF16)
    AFC = [AFCT.ap()[:, 16 * i:16 * (i + 1)] for i in range(8)]
    APJT = A("apjt", [128, 16 * 32], BF16)
    APJ = [APJT.ap()[:, 16 * i:16 * (i + 1)] for i in range(32)]
    B2FP = A("b2fcp", [128, 2048], BF16)
    B2PP = A("b2pjp", [128, 512], BF16)
    xa1sb = A("xa1sb", [48, 256], BF16)
    xa2sb = A("xa2sb", [48, 256], BF16)

    CA1b = A("CA1b", [128, D], BF16)
    SA1b = A("SA1b", [128, D], BF16)
    CBb = A("CBb", [128, D], BF16)
    SBb = A("SBb", [128, D], BF16)

    bfccol = A("bfccol", [128, 32], F32)
    bpjcol = A("bpjcol", [128, 8], F32)
    gbcol = A("gbcol", [128, 8], F32)
    gacol = A("gacol", [128, 8], F32)
    spgb = A("spgb", [128, 8], F32)
    gbbp = A("gbbp", [128, 8], F32)
    scol = A("scol", [128, 12], F32)
    ccol = A("ccol", [128, 8], BF16)
    mr1 = A("mr1", [128, 8], F32)
    mr2 = A("mr2", [128, 16], F32)
    stat12 = A("stat12", [128, 12], F32)
    Zcols = A("Zcols", [128, 64], F32)
    rzcols = A("rzcols", [128, 64], F32)
    cmaxs = A("cmaxs", [16, 2], F32)
    cmaxT = A("cmaxT", [1, 16], F32)
    negc = A("negc", [128, 16], F32)
    dencol = A("dencol", [128, 4], F32)
    epscol = A("epscol", [128, 1], F32)
    padstage = A("padstage", [128, 128], F32)
    cmrow = A("cmrow", [1, 16], BF16)

    PB = [nc.alloc_psum_tensor(f"pb{i}", [128, 512], F32) for i in range(6)]
    PT = [nc.alloc_psum_tensor(f"pt{i}", [128, 1024], BF16) for i in range(2)]

    dSY = Sem(nc, "dSY")
    dSC = Sem(nc, "dSC")
    dGW = Sem(nc, "dGW")
    dGO = Sem(nc, "dGO")
    sP = Sem(nc, "sP")
    sV = Sem(nc, "sV")
    sS = Sem(nc, "sS")
    sG = Sem(nc, "sG")
    sQC = Sem(nc, "sQC")
    sWM = Sem(nc, "sWM")

    P = Prog()
    opn = P.op
    wt = P.wait

    dump_aps = {}
    def dump(name, ap, shape, dtype=F32):
        if name not in dumps:
            return
        d = nc.dram_tensor("dbg_" + name, shape, dtype, kind="ExternalOutput").ap()
        wt("gpsimd", dGO, dGO.n)
        wt("gpsimd", dGW, dGW.n)
        opn("gpsimd", lambda e, d=d, ap=ap: e.dma_start(out=d, in_=ap), dGO, 16)
        for e in ("tensor", "vector", "scalar", "sync"):
            wt(e, dGO)

    xev_load = [[None] * 4 for _ in range(2)]
    for t in range(4):
        xev_load[0][t] = opn("sync", lambda e, t=t: e.dma_start(
            out=XEV[t].ap(), in_=xev[0, 128 * t:128 * (t + 1), :]), dSY, 16)

    small_loads = []
    def L(dst, src):
        small_loads.append(opn("sync", lambda e, d=dst, s=src: e.dma_start(
            out=d, in_=s), dSY, 16))

    L(bfccol.ap(), bfc_c)
    L(bpjcol.ap(), bpj_c)
    L(csil, c1)
    for i in range(8):
        L(QSC.ap()[:, 16 * i:16 * (i + 1)], afcT[128 * i:128 * (i + 1), :])
    for i in range(32):
        L(QSC.ap()[:, 128 + 16 * i:128 + 16 * (i + 1)], apjT[128 * i:128 * (i + 1), :])
    tiny_done = dSY.n

    wload = []
    for i in range(16):
        dc, hf = divmod(i, 2)
        if i >= 2:
            wt("sync", sQC, i - 1)
        wload.append(opn("sync", lambda e, dc=dc, hf=hf, i=i: e.dma_start(
            out=qbuf[i % 2], in_=wfcT[128 * dc:128 * (dc + 1),
                                      2048 * hf:2048 * (hf + 1)]), dSY, 16))
    for i in range(16):
        dc, hf = divmod(i, 2)
        j = 16 + i
        wt("sync", sQC, j - 1)
        wload.append(opn("sync", lambda e, dc=dc, hf=hf, i=i: e.dma_start(
            out=qbuf[i % 2], in_=wfcT[128 * dc:128 * (dc + 1),
                                      2048 * hf:2048 * (hf + 1)]), dSY, 16))
    wpjr = wpjT.rearrange("(c p) d -> p c d", p=128)
    for i in range(16):
        j = 32 + i
        wt("sync", sQC, j - 1)
        wload.append(opn("sync", lambda e, i=i: e.dma_start(
            out=qbuf[i % 2], in_=wpjr[:, 2 * i:2 * (i + 1), :]), dSY, 16))
    for i in range(16):
        j = 48 + i
        wt("sync", sQC, j - 1)
        wload.append(opn("sync", lambda e, i=i: e.dma_start(
            out=qbuf[i % 2], in_=wpjr[:, 2 * i:2 * (i + 1), :]), dSY, 16))

    opn("vector", lambda e: e.memset(B2FP.ap()[:], 0.0))
    opn("vector", lambda e: e.memset(B2PP.ap()[:], 0.0), sV)
    b2z_v = sV.n
    opn("gpsimd", lambda e: e.iota(iotat, [[1, 128]], channel_multiplier=-1,
                                   allow_small_or_imprecise_dtypes=True), sG)
    wt("gpsimd", sV, b2z_v)
    for fcx in range(32):
        if fcx >= 6:
            wt("gpsimd", dGW, (fcx - 5) * 16)
        opn("gpsimd", lambda e, fcx=fcx: e.dma_start(
            out=B2FP.ap()[32 * (fcx % 2):32 * (fcx % 2) + 16,
                          128 * (fcx // 2):128 * (fcx // 2) + 128],
            in_=b2fcT[0:16, 128 * fcx:128 * (fcx + 1)]), dGW, 16)
    for dc in range(8):
        wt("gpsimd", dGW, (27 + dc) * 16)
        opn("gpsimd", lambda e, dc=dc: e.dma_start(
            out=B2PP.ap()[32 * (dc % 2):32 * (dc % 2) + 16,
                          128 * (dc // 2):128 * (dc // 2) + 128],
            in_=b2pjT[0:16, 128 * dc:128 * (dc + 1)]), dGW, 16)
    b2_loads = dGW.n
    wt("vector", sG)
    opn("vector", lambda e: e.tensor_single_scalar(identf[:], iotat, 0.0,
                                                   ALU.is_equal))
    opn("vector", lambda e: e.tensor_copy(identb[:], identf[:]))
    opn("vector", lambda e: e.memset(onescolf[:], 1.0))
    opn("vector", lambda e: e.memset(onescolb[:], 1.0))
    opn("vector", lambda e: e.memset(epscol.ap()[:], EPS))
    opn("vector", lambda e: e.memset(padstage.ap()[:], 0.0), sV)
    ident_v = sV.n

    wt("scalar", dSY, tiny_done)
    opn("scalar", lambda e: e.activation(csil, csil, AF.Silu), sS)
    wt("tensor", sS)
    wt("tensor", sV, ident_v)
    for k in range(8):
        opn("tensor", lambda e, k=k: e.matmul(
            PB[4].ap()[:, k:k + 1], csil[0:1, 128 * k:128 * (k + 1)],
            identf[0:1, 0:1], is_transpose=True, skip_group_check=True,
            start=True, stop=True), sP if k == 7 else None)
    wt("vector", sP)
    opn("vector", lambda e: e.tensor_copy(ccol.ap()[:], PB[4].ap()[:, 0:8]))
    opn("vector", lambda e: e.tensor_copy(AFCT.ap()[:], afc_stage))
    opn("vector", lambda e: e.tensor_copy(APJT.ap()[:], apj_stage), sV)
    ccol_v = sV.n
    wt("gpsimd", sV, ccol_v)
    dump("ccol", ccol.ap(), [128, 8], BF16)
    dump("csil", csil, [1, 1024])

    wt("gpsimd", dGW, b2_loads)
    for k in range(16):
        half, dc = divmod(k, 8)
        if k >= 2:
            wt("gpsimd", sWM, k - 1)
        for hh in range(2):
            m = 2 * k + hh
            if m >= 3:
                wt("gpsimd", dGW, b2_loads + (m - 2) * 16)
            opn("gpsimd", lambda e, half=half, dc=dc, k=k, hh=hh: e.dma_start(
                out=wmodbuf[k % 2][64 * hh:64 * (hh + 1), :],
                in_=wmod[128 * dc + 64 * hh:128 * dc + 64 * (hh + 1),
                         3072 * half:3072 * (half + 1)]), dGW, 16)
    for half in range(2):
        if half == 1:
            wt("tensor", sV)
        for dc in range(8):
            k = half * 8 + dc
            wt("tensor", dGW, b2_loads + (k + 1) * 32)
            wt("tensor", sV, ccol_v)
            for g in range(6):
                opn("tensor", lambda e, k=k, dc=dc, g=g: e.matmul(
                    PB[g].ap()[0:1, :], ccol.ap()[:, dc:dc + 1],
                    wmodbuf[k % 2][:, 512 * g:512 * (g + 1)],
                    start=(dc == 0), stop=(dc == 7)),
                    sWM if g == 5 else None)
        wt("vector", sWM, (half + 1) * 8)
        for g in range(6):
            opn("vector", lambda e, h=half, g=g: e.tensor_copy(
                modrow[0:1, 3072 * h + 512 * g:3072 * h + 512 * (g + 1)],
                PB[g].ap()[0:1, :]), sV if g == 5 else None)

    wt("vector", sV)
    for off in (2048, 5120):
        opn("vector", lambda e, o=off: e.tensor_scalar(
            QSC.ap()[0:1, 0:1024], modrow[0:1, o:o + 1024], 0.0, 2.0,
            ALU.is_ge, ALU.mult))
        opn("vector", lambda e: e.tensor_scalar(
            QSC.ap()[0:1, 0:1024], QSC.ap()[0:1, 0:1024], 1.0, None,
            ALU.subtract), sV)
        wt("scalar", sV)
        opn("scalar", lambda e, o=off: e.activation(
            modrow[0:1, o:o + 1024], modrow[0:1, o:o + 1024], AF.Abs), sS)
        wt("vector", sS)
        opn("vector", lambda e, o=off: e.tensor_scalar(
            modrow[0:1, o:o + 1024], modrow[0:1, o:o + 1024], 0.25, None,
            ALU.max))
        opn("vector", lambda e, o=off: e.tensor_tensor(
            modrow[0:1, o:o + 1024], QSC.ap()[0:1, 0:1024],
            modrow[0:1, o:o + 1024], ALU.mult))
    for off in (1024, 4096):
        opn("vector", lambda e, o=off: e.tensor_scalar(
            modrow[0:1, o:o + 1024], modrow[0:1, o:o + 1024], 1.0, None, ALU.add))
    opn("vector", lambda e: e.tensor_copy(modrow_b, modrow), sV)
    modrow_v = sV.n

    wt("gpsimd", sV, modrow_v) if "modrow" in dumps else None
    dump("modrow", modrow, [1, 6144])
    wt("tensor", sV, modrow_v)
    for k in range(8):
        opn("tensor", lambda e, k=k: e.matmul(
            PB[5].ap()[:, k:k + 1],
            modrow[0:1, 2048 + 128 * k:2048 + 128 * (k + 1)],
            identf[0:1, 0:1], is_transpose=True, skip_group_check=True,
            start=True, stop=True), None)
    for k in range(8):
        opn("tensor", lambda e, k=k: e.matmul(
            PB[5].ap()[:, 8 + k:9 + k],
            modrow[0:1, 5120 + 128 * k:5120 + 128 * (k + 1)],
            identf[0:1, 0:1], is_transpose=True, skip_group_check=True,
            start=True, stop=True), sP if k == 7 else None)
    wt("vector", sP)
    opn("vector", lambda e: e.tensor_copy(gacol.ap()[:], PB[5].ap()[:, 0:8]))
    opn("vector", lambda e: e.tensor_copy(gbcol.ap()[:], PB[5].ap()[:, 8:16]), sV)

    def bcast(dst, off):
        wt("tensor", sV)
        for j in range(2):
            opn("tensor", lambda e, o=off, j=j: e.matmul(
                PB[j].ap()[:, :], onescolb[:],
                modrow_b[0:1, o + 512 * j:o + 512 * (j + 1)],
                start=True, stop=True), sP)
        wt("vector", sP)
        for j in range(2):
            opn("vector", lambda e, d=dst, j=j: e.tensor_copy(
                d.ap()[:, 512 * j:512 * (j + 1)], PB[j].ap()[:, :]),
                sV if j == 1 else None)

    bcast(SA1b, 0)
    bcast(CA1b, 1024)
    bcast(SBb, 3072)
    bcast(CBb, 4096)
    mods_done_v = sV.n
    wt("gpsimd", sV, mods_done_v)
    dump("CA1b", CA1b.ap(), [128, 1024], BF16)
    dump("SA1b", SA1b.ap(), [128, 1024], BF16)
    dump("gacol", gacol.ap(), [128, 8])
    dump("gbcol", gbcol.ap(), [128, 8])

    pump = {"next": 0, "fc_ok": False, "pj_ok": False}

    def amax_chunk(kind, i):
        li = i if kind == 0 else 32 + i
        col = 0 if kind == 0 else 3
        wt("vector", dSY, wload[li])
        if i == 0:
            opn("vector", lambda e, c=col, i=i: e.reduce_max(
                scol.ap()[:, c:c + 1], qbuf[i % 2], axis=AX.X,
                apply_absolute_value=True), sV)
            wt("vector", sV)
            opn("vector", lambda e, c=col: e.tensor_copy(
                scol.ap()[0:1, 10:11], scol.ap()[0:1, 10:11]), sQC)
        else:
            opn("vector", lambda e, i=i: e.reduce_max(
                scol.ap()[:, 6:7], qbuf[i % 2], axis=AX.X,
                apply_absolute_value=True), sV)
            wt("vector", sV)
            opn("vector", lambda e, c=col: e.tensor_tensor(
                scol.ap()[:, c:c + 1], scol.ap()[:, c:c + 1],
                scol.ap()[:, 6:7], ALU.max), sQC)

    def finish_amax(kind, pb):
        c0, cs, cr = (0, 1, 2) if kind == 0 else (3, 4, 5)
        opn("vector", lambda e, c0=c0: e.tensor_copy(
            padstage.ap()[:, 0:1], scol.ap()[:, c0:c0 + 1]), sV)
        wt("tensor", sV)
        opn("tensor", lambda e, pb=pb: e.transpose(
            pb.ap()[:, 0:128], padstage.ap()[:], identf[:]))
        opn("tensor", lambda e, pb=pb: e.transpose(
            pb.ap()[:, 0:128], padstage.ap()[:], identf[:]), sP)
        wt("vector", sP)
        opn("vector", lambda e, pb=pb: e.reduce_max(
            cmaxs.ap()[0:1, 1:2], pb.ap()[0:1, 0:128], axis=AX.X), sV)
        wt("tensor", sV)
        opn("tensor", lambda e, pb=pb: e.matmul(
            pb.ap()[:, 256:257], onescolf[:], cmaxs.ap()[0:1, 1:2],
            start=True, stop=True, skip_group_check=True))
        opn("tensor", lambda e, pb=pb: e.matmul(
            pb.ap()[:, 256:257], onescolf[:], cmaxs.ap()[0:1, 1:2],
            start=True, stop=True, skip_group_check=True), sP)
        wt("vector", sP)
        opn("vector", lambda e, c0=c0, pb=pb: e.tensor_copy(
            scol.ap()[:, c0:c0 + 1], pb.ap()[:, 256:257]), sV)
        wt("vector", sV)
        opn("vector", lambda e, c0=c0, cs=cs: e.tensor_scalar(
            scol.ap()[:, cs:cs + 1], scol.ap()[:, c0:c0 + 1], 1.0 / 127.0,
            None, ALU.mult))
        opn("vector", lambda e, c0=c0: e.reciprocal(
            scol.ap()[:, 6:7], scol.ap()[:, c0:c0 + 1]), sV)
        wt("vector", sV)
        opn("vector", lambda e, cr=cr: e.tensor_scalar(
            scol.ap()[:, cr:cr + 1], scol.ap()[:, 6:7], 127.0, None, ALU.mult),
            sV)

    def scale_b2(kind):
        cs, ct = (1, 8) if kind == 0 else (4, 9)
        opn("vector", lambda e, cs=cs: e.reciprocal(
            scol.ap()[:, 6:7], scol.ap()[:, cs:cs + 1]), sV)
        wt("vector", sV)
        opn("vector", lambda e, ct=ct: e.tensor_scalar(
            scol.ap()[:, ct:ct + 1], scol.ap()[:, 6:7], 2.0, None, ALU.mult), sV)
        if kind == 1:
            wt("vector", sV)
            opn("vector", lambda e: e.tensor_scalar(
                spgb.ap()[:], gbcol.ap()[:], scol.ap()[:, 4:5], None, ALU.mult))
            opn("vector", lambda e: e.tensor_tensor(
                gbbp.ap()[:], gbcol.ap()[:], bpjcol.ap()[:], ALU.mult), sV)

    def quant_chunk(kind, i):
        li = (16 if kind == 0 else 48) + i
        cr = 2 if kind == 0 else 5
        wt("vector", dSY, wload[li])
        wt("scalar", dSY, wload[li])
        for sub in range(4):
            srcap = qbuf[i % 2][:, 512 * sub:512 * (sub + 1)]
            if kind == 0:
                dc, hf = divmod(i, 2)
                dst = WFC[dc].ap()[:, 2048 * hf + 512 * sub:
                                   2048 * hf + 512 * (sub + 1)]
            else:
                fcx = 2 * i + sub // 2
                dst = WPJ[fcx][:, 512 * (sub % 2):512 * (sub % 2 + 1)]
            wt("scalar", sV, max(pump.get("last_tt", 0),
                                 pump.get("rinv_v", {}).get(kind, 0)))
            opn("scalar", lambda e, s=srcap, cr=cr: e.activation(
                qa_s, s, AF.Abs, scale=scol.ap()[:, cr:cr + 1]), sS)
            wt("vector", sS)
            opn("vector", lambda e: e.tensor_scalar(
                ti_s, qa_s, CLIP, 0.5, ALU.min, ALU.add), sV)
            opn("scalar", lambda e, s=srcap: e.activation(sg_s, s, AF.Sign), sS)
            wt("vector", sS)
            opn("vector", lambda e, d=dst: e.tensor_tensor(
                d, sg_s, ti_s, ALU.mult), sV)
            pump["last_tt"] = sV.n
        opn("vector", lambda e: e.tensor_copy(
            scol.ap()[0:1, 10:11], scol.ap()[0:1, 10:11]), sQC)

    def pump_quant(k):
        for _ in range(k):
            j = pump["next"]
            if j >= 64:
                return
            if j < 16:
                amax_chunk(0, j)
            elif j < 32:
                if not pump["fc_ok"]:
                    return
                quant_chunk(0, j - 16)
            elif j < 48:
                amax_chunk(1, j - 32)
            else:
                if not pump["pj_ok"]:
                    return
                quant_chunk(1, j - 48)
            pump["next"] = j + 1

    def ln_stats(src_ap, mr_ap):
        opn("vector", lambda e, s=src_ap: e.bn_stats(stat12.ap()[:, 0:6],
                                                     s[:, 0:512]))
        opn("vector", lambda e, s=src_ap: e.bn_stats(stat12.ap()[:, 6:12],
                                                     s[:, 512:1024]), sV)
        wt("vector", sV)
        opn("vector", lambda e, m=mr_ap: e.bn_aggr(m, stat12.ap()[:, 0:12]), sV)
        wt("scalar", sV)
        opn("scalar", lambda e, m=mr_ap: e.activation(
            m[:, 1:2], m[:, 1:2], AF.Ln, bias=epscol.ap()[:, 0:1]), sS)
        wt("scalar", sS)
        opn("scalar", lambda e, m=mr_ap: e.activation(
            m[:, 1:2], m[:, 1:2], AF.Exp, scale=-0.5), sS)

    exp_inc_of = {}
    gelu_inc = {}
    tb_cons = [{}, {}]

    def seg_attention(seg, do_pump, astage=9):
        def pq(k):
            if do_pump:
                pump_quant(k)

        for t in range(4):
            wt("vector", dSY, xev_load[seg][t])
            wt("vector", sV, mods_done_v)
            ln_stats(XEV[t].ap(), mr1.ap()[:, 2 * t:2 * t + 2])
            wt("vector", sS)
            opn("vector", lambda e, t=t: e.tensor_scalar(
                mscr, XEV[t].ap(), mr1.ap()[:, 2 * t:2 * t + 1],
                mr1.ap()[:, 2 * t + 1:2 * t + 2], ALU.subtract, ALU.mult))
            opn("vector", lambda e: e.tensor_tensor(
                q2scr, mscr, CA1b.ap(), ALU.mult))
            opn("vector", lambda e, t=t: e.tensor_tensor(
                heven[t], q2scr, SA1b.ap(), ALU.add), sV)
            pq(3)
        hev_done_v = sV.n
        if astage <= 1:
            return
        if seg == dbgseg:
            wt("gpsimd", sV, hev_done_v)
            dump("mr1", mr1.ap(), [128, 8])
            for _t in range(4):
                dump(f"heven{_t}", heven[_t], [128, 1024], BF16)

        for t in range(4):
            wt("scalar", sV, hev_done_v)
            if t > 0:
                wt("scalar", sV)
            opn("scalar", lambda e, t=t: e.activation(q2scr, heven[t], AF.Square),
                sS)
            wt("vector", sS)
            opn("vector", lambda e, t=t: e.reduce_sum(
                Zcols.ap()[:, 16 * t:16 * (t + 1)],
                q2scr.rearrange("p (h k) -> p h k", h=16), axis=AX.X), sV)
        zred_v = sV.n
        pq(4)
        if seg == dbgseg:
            wt("gpsimd", sV)
            dump("ncolsd", Zcols.ap(), [128, 64])
        if do_pump:
            finish_amax(0, PB[4])
            scale_b2(0)
            pump.setdefault("rinv_v", {})[0] = sV.n
            pump["fc_ok"] = True
        wt("vector", sV, zred_v)
        if seg == dbgseg and "zsnap" in dumps:
            opn("vector", lambda e: e.tensor_copy(
                mscr[:, 0:64], Zcols.ap()[:, 0:64]), sV)
            wt("gpsimd", sV)
            dump("zsnap", mscr[:, 0:64], [128, 64])
        opn("vector", lambda e: e.tensor_tensor(
            mscr[:, 0:16], Zcols.ap()[:, 0:16], Zcols.ap()[:, 16:32],
            ALU.max))
        opn("vector", lambda e: e.tensor_tensor(
            mscr[:, 16:32], Zcols.ap()[:, 32:48], Zcols.ap()[:, 48:64],
            ALU.max), sV)
        wt("vector", sV)
        opn("vector", lambda e: e.tensor_tensor(
            Zcols.ap()[:, 0:16], mscr[:, 0:16], mscr[:, 16:32],
            ALU.max), sV)
        wt("vector", sV)
        if seg == dbgseg and "zsnap" in dumps:
            opn("vector", lambda e: e.tensor_copy(
                mscr[:, 64:80], Zcols.ap()[:, 0:16]), sV)
            wt("gpsimd", sV)
            dump("zpost", mscr[:, 64:80], [128, 16])
        opn("vector", lambda e: e.tensor_copy(
            padstage.ap()[:, 0:16], Zcols.ap()[:, 0:16]), sV)
        if seg == dbgseg:
            wt("gpsimd", sV)
            dump("padsd", padstage.ap()[:, 0:16], [128, 16])
            dump("zcol16", Zcols.ap()[:, 0:16], [128, 16])
        wt("tensor", sV)
        opn("tensor", lambda e: e.transpose(
            PB[0].ap()[:, 0:128], padstage.ap()[:], identf[:]))
        opn("tensor", lambda e: e.transpose(
            PB[0].ap()[:, 0:128], padstage.ap()[:], identf[:]), sP)
        wt("vector", sP)
        opn("vector", lambda e: e.reduce_max(
            cmaxs.ap()[0:16, 0:1], PB[0].ap()[0:16, 0:128], axis=AX.X), sV)
        if seg == dbgseg and "pbsnap" in dumps:
            opn("vector", lambda e: e.tensor_copy(
                mscr[:, 128:256], PB[0].ap()[:, 0:128]), sV)
            wt("gpsimd", sV)
            dump("pbsnap", mscr[:, 128:256], [128, 128])
            dump("identfd", identf.ap()[:, :], [128, 128])
            dump("identbd", identb.ap()[:, :], [128, 128], BF16)
            dump("padfull", padstage.ap()[:, :], [128, 128])
        wt("vector", sV)
        opn("vector", lambda e: e.tensor_copy(
            padstage.ap()[0:16, 0:1], cmaxs.ap()[0:16, 0:1]), sV)
        if seg == dbgseg:
            wt("gpsimd", sV)
            dump("cmaxsd", cmaxs.ap()[0:16, 0:1], [16, 1])
        wt("tensor", sV)
        opn("tensor", lambda e: e.transpose(
            PB[1].ap()[:, 0:128], padstage.ap()[:], identf[:]))
        opn("tensor", lambda e: e.transpose(
            PB[1].ap()[:, 0:128], padstage.ap()[:], identf[:]), sP)
        wt("vector", sP)
        opn("vector", lambda e: e.tensor_copy(
            cmrow.ap()[:], PB[1].ap()[0:1, 0:16]), sV)
        if seg == dbgseg:
            wt("gpsimd", sV)
            dump("cmrowd", cmrow.ap(), [1, 16], BF16)
        wt("tensor", sV)
        opn("tensor", lambda e: e.matmul(
            PB[1].ap()[:, 128:144], onescolb[:], cmrow.ap()[:],
            start=True, stop=True, skip_group_check=True))
        opn("tensor", lambda e: e.matmul(
            PB[1].ap()[:, 128:144], onescolb[:], cmrow.ap()[:],
            start=True, stop=True, skip_group_check=True), sP)
        wt("vector", sP)
        opn("vector", lambda e: e.tensor_scalar(
            negc.ap()[:], PB[1].ap()[:, 128:144], -0.125, None, ALU.mult), sV)
        negc_v = sV.n
        if astage <= 2:
            return
        if seg == dbgseg:
            wt("gpsimd", sV, negc_v)
            dump("negc", negc.ap(), [128, 16])

        qT_evac_v = [0, 0]
        qT_reader_p = [0, 0]
        av_read_v = [0, 0]
        pairsb_read_p = [0]
        tb_state = {"B": 0}
        av_done_p = {}
        rz_v = {}

        def tbank_alloc():
            B = tb_state["B"]
            tb_state["B"] = B + 1
            bank = PT[B % 2]
            half = (B // 2) % 2
            for sem, n in tb_cons[B % 2].items():
                wt("tensor", sem, n)
            return bank, 512 * half, B % 2

        def tbank_consumed(bankidx, sem):
            tb_cons[bankidx][sem] = sem.n

        def qt_make(h):
            gr = h % 2
            wt("tensor", sV, hev_done_v)
            bank, off, bi = tbank_alloc()
            if astage == 211:
                return
            for tb in range(4):
                opn("tensor", lambda e, h=h, tb=tb, bank=bank, off=off: e.matmul(
                    bank.ap()[0:64, off + 128 * tb:off + 128 * (tb + 1)],
                    heven[tb][:, 64 * h:64 * (h + 1)], identb[:],
                    is_transpose=True, skip_group_check=True,
                    start=True, stop=True), sP if tb == 3 else None)
            tq = sP.n
            if astage == 212:
                qT_evac_v[gr] = sV.n
                return
            wt("vector", sP, tq)
            wt("vector", sP, qT_reader_p[gr])
            opn("vector", lambda e, gr=gr, bank=bank, off=off: e.tensor_copy(
                qT[gr], bank.ap()[0:64, off:off + 512]), sV)
            qT_evac_v[gr] = sV.n
            tbank_consumed(bi, sV)

        if (astage > 2 and astage not in (211, 212)) or astage in (21, 25, 211, 212):
            qt_make(0)
            qt_make(1)
        if astage in (21, 211, 212):
            qt_make(2)
            qt_make(3)

        nheads = 0 if (astage <= 2 or astage in (21, 211, 212)) else (
            1 if astage in (22, 23, 24) else (2 if astage == 25 else NH))
        for h in range(nheads):
            p, sub = divmod(h, 2)
            gr = h % 2
            wt("tensor", sV, qT_evac_v[gr])
            wt("tensor", sV, negc_v)
            for ib in range(4):
                u = 4 * h + ib
                prev = exp_inc_of.get((seg, u - 3))
                if prev:
                    wt("tensor", sS, prev)
                opn("tensor", lambda e, u=u, ib=ib, gr=gr: e.matmul(
                    PB[u % 3].ap()[:, :], qT[gr][:, 128 * ib:128 * (ib + 1)],
                    qT[gr][:, :], start=True, stop=True), sP)
                if astage == 22:
                    continue
                wt("scalar", sP, sP.n)
                prevav = av_done_p.get(h - 2)
                if prevav:
                    wt("scalar", sP, prevav)
                opn("scalar", lambda e, h=h, ib=ib, u=u: e.activation(
                    Et[h % 2][ib], PB[u % 3].ap()[:, :], AF.Exp,
                    bias=negc.ap()[:, h:h + 1], scale=0.125,
                    accum_out=Zcols.ap()[:, 4 * h + ib:4 * h + ib + 1]), sS)
                exp_inc_of[(seg, u)] = sS.n
            qT_reader_p[gr] = sP.n
            if astage == 22:
                continue
            wt("vector", sS)
            opn("vector", lambda e, h=h: e.reciprocal(
                rzcols.ap()[:, 4 * h:4 * h + 4], Zcols.ap()[:, 4 * h:4 * h + 4]),
                sV)
            rz_v[h] = sV.n
            if h + 2 < nheads or (nheads == NH and h + 2 < NH):
                qt_make(h + 2)
            if astage == 23:
                continue
            wt("tensor", sS, exp_inc_of[(seg, 4 * h + 3)])
            wt("tensor", sV, av_read_v[p % 2])
            for jb in range(4):
                opn("tensor", lambda e, h=h, jb=jb, p=p, sub=sub: e.matmul(
                    PB[4 + p % 2].ap()[64 * sub:64 * (sub + 1), :],
                    heven[jb][:, 64 * h:64 * (h + 1)], Et[h % 2][jb],
                    start=(jb == 0), stop=(jb == 3),
                    skip_group_check=(sub == 1)), sP if jb == 3 else None)
            av_done_p[h] = sP.n
            if sub == 1:
                av_p = sP.n
                wt("vector", sP, av_p)
                wt("vector", sP, pairsb_read_p[0])
                opn("vector", lambda e, p=p: e.tensor_scalar(
                    pairsb[:, :], PB[4 + p % 2].ap()[:, :],
                    gacol.ap()[:, p:p + 1], None, ALU.mult), sV)
                av_read_v[p % 2] = sV.n
                pv = sV.n
                wt("tensor", sV, pv)
                bank, off, bi = tbank_alloc()
                for b in range(4):
                    opn("tensor", lambda e, b=b, bank=bank, off=off: e.matmul(
                        bank.ap()[:, off + 128 * b:off + 128 * (b + 1)],
                        pairsb[:, 128 * b:128 * (b + 1)], identb[:],
                        is_transpose=True, skip_group_check=True,
                        start=True, stop=True), sP)
                    tp = sP.n
                    wt("scalar", sP, tp)
                    wt("scalar", sV, rz_v[2 * p + 1])
                    for s2 in range(2):
                        opn("scalar", lambda e, b=b, p=p, s2=s2, bank=bank,
                            off=off: e.activation(
                                aos[b][:, 128 * p + 64 * s2:
                                       128 * p + 64 * (s2 + 1)],
                                bank.ap()[:, off + 128 * b + 64 * s2:
                                          off + 128 * b + 64 * (s2 + 1)],
                                AF.Copy,
                                scale=rzcols.ap()[:, 4 * (2 * p + s2) + b:
                                                  4 * (2 * p + s2) + b + 1]),
                            sS)
                tbank_consumed(bi, sS)
                pairsb_read_p[0] = sP.n
            pq(2)

        if do_pump:
            finish_amax(1, PB[4])
            scale_b2(1)
            pump.setdefault("rinv_v", {})[1] = sV.n
            pump["pj_ok"] = True
        if astage <= 3:
            return

        aos_done_s = sS.n
        if seg == dbgseg:
            wt("gpsimd", sS, aos_done_s)
            wt("gpsimd", sV)
            dump("Zcols", Zcols.ap(), [128, 64])
            dump("rzcols", rzcols.ap(), [128, 64])
            for _b in range(4):
                dump(f"aos{_b}", aos[_b], [128, 1024], BF16)
        for b in range(4):
            wt("vector", sS, aos_done_s)
            opn("vector", lambda e, b=b: e.reduce_sum(
                dencol.ap()[:, b:b + 1], aos[b][:, :], axis=AX.X), sV)
            wt("vector", sV)
            opn("vector", lambda e, b=b: e.reciprocal(
                scol.ap()[:, 7:8], dencol.ap()[:, b:b + 1]), sV)
            wt("vector", sV)
            opn("vector", lambda e, b=b: e.tensor_tensor(
                dencol.ap()[:, b:b + 1], dencol.ap()[:, b:b + 1],
                scol.ap()[:, 7:8], ALU.mult), sV)
            wt("vector", sV)
            opn("vector", lambda e, b=b: e.tensor_scalar(
                t2f, aos[b][:, :], dencol.ap()[:, b:b + 1], None, ALU.mult))
            opn("vector", lambda e, b=b: e.tensor_tensor(
                XEV[b].ap(), XEV[b].ap(), t2f, ALU.add), sV)
            pq(2)

    def ffn_tile(src_aps, mr_off, outs, do_stats, dbg=False):
        for sub in range(2):
            if do_stats:
                ln_stats(src_aps[sub],
                         mr2.ap()[:, mr_off + 2 * sub:mr_off + 2 * sub + 2])
            wt("vector", sS)
            opn("vector", lambda e, sub=sub: e.tensor_scalar(
                mscr, src_aps[sub],
                mr2.ap()[:, mr_off + 2 * sub:mr_off + 2 * sub + 1],
                mr2.ap()[:, mr_off + 2 * sub + 1:mr_off + 2 * sub + 2],
                ALU.subtract, ALU.mult))
            opn("vector", lambda e: e.tensor_tensor(
                q2scr, mscr, CBb.ap(), ALU.mult))
            opn("vector", lambda e, sub=sub: e.tensor_tensor(
                h2tm[sub], q2scr, SBb.ap(), ALU.add), sV)
        mod_v = sV.n
        for sem, n in list(tb_cons[0].items()) + list(tb_cons[1].items()):
            wt("tensor", sem, n)
        if dbg:
            wt("gpsimd", sV, mod_v)
            dump("h2tm0", h2tm[0], [128, 1024], BF16)
            dump("h2tm1", h2tm[1], [128, 1024], BF16)
            dump("mr2", mr2.ap()[:, 0:4], [128, 4])
        wt("tensor", sV, mod_v)
        for fill in range(2):
            if fill == 1:
                wt("tensor", sV)
            for k in range(4):
                dc = fill * 4 + k
                for sub in range(2):
                    opn("tensor", lambda e, dc=dc, k=k, sub=sub: e.matmul(
                        PT[0].ap()[:, 256 * k + 128 * sub:
                                   256 * k + 128 * (sub + 1)],
                        h2tm[sub][:, 128 * dc:128 * (dc + 1)], identb[:],
                        is_transpose=True, skip_group_check=True,
                        start=True, stop=True),
                        sP if (k == 3 and sub == 1) else None)
            fp = sP.n
            wt("vector", sP, fp)
            for k in range(4):
                dc = fill * 4 + k
                opn("vector", lambda e, dc=dc, k=k: e.tensor_copy(
                    h2T[dc][:, :], PT[0].ap()[:, 256 * k:256 * (k + 1)]),
                    sV if k == 3 else None)
        h2T_v = sV.n
        tb_cons[0][sV] = h2T_v
        if dbg:
            wt("gpsimd", sV, h2T_v)
            dump("h2T0", h2T[0], [128, 256], BF16)
            dump("h2T7", h2T[7], [128, 256], BF16)
        wt("tensor", sV, h2T_v)
        for dc in range(8):
            opn("tensor", lambda e, dc=dc: e.matmul(
                PB[0].ap()[0:16, 0:256], AFC[dc], h2T[dc][:, :],
                start=(dc == 0), stop=(dc == 7)), sP if dc == 7 else None)
        wt("vector", sP)
        opn("vector", lambda e: e.tensor_scalar(
            xa1sb.ap()[0:16, :], PB[0].ap()[0:16, 0:256], scol.ap()[0:16, 8:9],
            None, ALU.mult))
        opn("vector", lambda e: e.tensor_scalar(
            xa1sb.ap()[32:48, :], PB[0].ap()[0:16, 0:256], scol.ap()[0:16, 8:9],
            None, ALU.mult), sV)
        xa1_v = sV.n
        if dbg:
            wt("gpsimd", sV, xa1_v)
            dump("xa1", xa1sb.ap()[0:16, :], [16, 256], BF16)
        wt("tensor", dGW, b2_loads)
        for fcx in range(32):
            bank = PB[1 + fcx % 2]
            wt("tensor", sV, xa1_v)
            prev = gelu_inc.get(("fc", fcx - 2))
            if prev:
                wt("tensor", sS, prev)
            for dc in range(8):
                opn("tensor", lambda e, fcx=fcx, dc=dc, bank=bank: e.matmul(
                    bank.ap()[:, 0:256], WFC[dc].ap()[:, 128 * fcx:128 * (fcx + 1)],
                    h2T[dc][:, :], start=(dc == 0), stop=False))
            opn("tensor", lambda e, fcx=fcx, bank=bank: e.matmul(
                bank.ap()[:, 0:256],
                B2FP.ap()[32 * (fcx % 2):32 * (fcx % 2) + 16,
                          128 * (fcx // 2):128 * (fcx // 2) + 128],
                xa1sb.ap()[32 * (fcx % 2):32 * (fcx % 2) + 16, :],
                start=False, stop=True), sP)
            mm = sP.n
            wt("scalar", sP, mm)
            opn("scalar", lambda e, fcx=fcx, bank=bank: e.activation(
                h1[fcx][:, :], bank.ap()[:, 0:256], AF.Gelu,
                scale=scol.ap()[:, 1:2], bias=bfccol.ap()[:, fcx:fcx + 1]), sS)
            gelu_inc[("fc", fcx)] = sS.n
        wt("tensor", sS, gelu_inc[("fc", 31)])
        for fcx in range(32):
            opn("tensor", lambda e, fcx=fcx: e.matmul(
                PB[0].ap()[0:16, 0:256], APJ[fcx], h1[fcx][:, :],
                start=(fcx == 0), stop=(fcx == 31)), sP if fcx == 31 else None)
        wt("vector", sP)
        opn("vector", lambda e: e.tensor_scalar(
            xa2sb.ap()[0:16, :], PB[0].ap()[0:16, 0:256], scol.ap()[0:16, 9:10],
            None, ALU.mult))
        opn("vector", lambda e: e.tensor_scalar(
            xa2sb.ap()[32:48, :], PB[0].ap()[0:16, 0:256], scol.ap()[0:16, 9:10],
            None, ALU.mult), sV)
        xa2_v = sV.n
        if dbg:
            wt("gpsimd", sS)
            wt("gpsimd", sV, xa2_v)
            dump("h1d0", h1[0], [128, 256], BF16)
            dump("h1d31", h1[31], [128, 256], BF16)
            dump("xa2", xa2sb.ap()[0:16, :], [16, 256], BF16)
        for dc in range(8):
            bank = PB[3 + dc % 2]
            wt("tensor", sV, xa2_v)
            prev = gelu_inc.get(("pj", dc - 2))
            if prev:
                wt("tensor", sV, prev[1])
            for fcx in range(32):
                opn("tensor", lambda e, dc=dc, fcx=fcx, bank=bank: e.matmul(
                    bank.ap()[:, 0:256], WPJ[fcx][:, 128 * dc:128 * (dc + 1)],
                    h1[fcx][:, :], start=(fcx == 0), stop=False))
            opn("tensor", lambda e, dc=dc, bank=bank: e.matmul(
                bank.ap()[:, 0:256],
                B2PP.ap()[32 * (dc % 2):32 * (dc % 2) + 16,
                          128 * (dc // 2):128 * (dc // 2) + 128],
                xa2sb.ap()[32 * (dc % 2):32 * (dc % 2) + 16, :],
                start=False, stop=True), sP)
            mm = sP.n
            wt("vector", sP, mm)
            opn("vector", lambda e, dc=dc, bank=bank: e.tensor_scalar(
                ffr[dc % 2], bank.ap()[:, 0:256], spgb.ap()[:, dc:dc + 1],
                gbbp.ap()[:, dc:dc + 1], ALU.mult, ALU.add), sV)
            gelu_inc[("pj", dc)] = ("v", sV.n)
            wt("tensor", sV, gelu_inc[("pj", dc)][1])
            for sub in range(2):
                opn("tensor", lambda e, dc=dc, sub=sub: e.matmul(
                    (PT[1] if sub == 0 else PT[0]).ap()[:, 128 * dc:128 * (dc + 1)],
                    ffr[dc % 2][:, 128 * sub:128 * (sub + 1)], identb[:],
                    is_transpose=True, skip_group_check=True,
                    start=True, stop=True),
                    sP if (dc == 7 and sub == 1) else None)
        stripes_p = sP.n
        for sub in range(2):
            wt("vector", sP, stripes_p)
            opn("vector", lambda e, sub=sub, src=src_aps[sub]: e.scalar_tensor_tensor(
                src, (PT[1] if sub == 0 else PT[0]).ap()[:, :],
                0.0, src, ALU.add, ALU.add), sV)
            rv = sV.n
            wt("gpsimd", sV, rv)
            wt("gpsimd", dGO, dGO.n)
            wt("gpsimd", dGW, dGW.n)
            opn("gpsimd", lambda e, o=outs[sub], src=src_aps[sub]: e.dma_start(
                out=o, in_=src), dGO, 16)
            tb_cons[1 - sub][sV] = rv

    seg_dump = {}
    xod_load = [[None] * 4 for _ in range(2)]

    def finish_dummy():
        wt("gpsimd", sP)
        for sg in range(NSEG):
            for t in range(4):
                wt("gpsimd", sV)
                wt("gpsimd", sS)
                wt("gpsimd", dGO, dGO.n)
                opn("gpsimd", lambda e, sg=sg, t=t: e.dma_start(
                    out=yev[sg, 128 * t:128 * (t + 1), :], in_=XEV[t].ap()),
                    dGO, 16)
                wt("gpsimd", dGO, dGO.n)
                opn("gpsimd", lambda e, sg=sg, t=t: e.dma_start(
                    out=yod[sg, 128 * t:128 * (t + 1), :], in_=XEV[t].ap()),
                    dGO, 16)
        P.q["gpsimd"].append(lambda e: e.wait_ge(dGO.h, dGO.n))
        with nc.Block() as blk:
            blk.sync(lambda e: [f(e) for f in P.q["sync"]])
            blk.tensor(lambda e: [f(e) for f in P.q["tensor"]])
            blk.vector(lambda e: [f(e) for f in P.q["vector"]])
            blk.scalar(lambda e: [f(e) for f in P.q["scalar"]])
            blk.gpsimd(lambda e: [f(e) for f in P.q["gpsimd"]])
        return nc, 0

    if stage == 1:
        return finish_dummy()
    if stage == 2:
        pump_quant(64)
        finish_amax(0, PB[4]); scale_b2(0)
        pump.setdefault("rinv_v", {})[0] = sV.n
        pump["fc_ok"] = True
        pump_quant(64)
        finish_amax(1, PB[4]); scale_b2(1)
        pump.setdefault("rinv_v", {})[1] = sV.n
        pump["pj_ok"] = True
        pump_quant(64)
        wt("gpsimd", sV)
        dump("scol", scol.ap()[:, 0:10], [128, 10])
        dump("wfc0", WFC[0].ap()[:, 0:1024], [128, 1024], BF16)
        dump("wpj0", WPJ[0], [128, 1024], BF16)
        dump("wpj31", WPJ[31], [128, 1024], BF16)
        dump("identfd2", identf.ap()[:, :], [128, 128])
        return finish_dummy()

    for seg in range(NSEG):
        if seg > 0:
            wt("vector", dGO)
        astage = {31: 1, 32: 2, 321: 21, 3211: 211, 3212: 212, 322: 22, 323: 23, 324: 24, 325: 25, 33: 3, 34: 9}.get(stage, 9)
        seg_attention(seg, do_pump=(seg == 0), astage=astage)
        if seg == 0:
            if stage in (31, 32, 321, 3211, 3212, 322, 323, 324, 325):
                finish_amax(0, PB[4]); scale_b2(0)
                pump.setdefault("rinv_v", {})[0] = sV.n
                pump["fc_ok"] = True
                pump_quant(64)
                finish_amax(1, PB[4]); scale_b2(1)
                pump.setdefault("rinv_v", {})[1] = sV.n
                pump["pj_ok"] = True
            pump_quant(64)
        if seg == dbgseg:
            wt("gpsimd", sV)
            dump("scol", scol.ap()[:, 0:10], [128, 10])
            dump("wfc0", WFC[0].ap()[:, 0:1024], [128, 1024], BF16)
            dump("wpj0", WPJ[0], [128, 1024], BF16)
            dump("b2fp", B2FP.ap()[0:48, :], [48, 2048], BF16)
            for _t in range(4):
                dump(f"x2ev{_t}", XEV[_t].ap(), [128, 1024])
        if stage in (3, 31, 32, 321, 3211, 3212, 322, 323, 324, 325, 33, 34):
            return finish_dummy()
        for tt in range(2):
            for sub in range(2):
                t = 2 * tt + sub
                ln_stats(XEV[t].ap(), mr2.ap()[:, 4 * tt + 2 * sub:
                                               4 * tt + 2 * sub + 2])
        for j in range(2):
            xod_load[seg][j] = opn("scalar", lambda e, seg=seg, j=j: e.dma_start(
                out=XOD[j], in_=xod[seg, 128 * j:128 * (j + 1), :]), dSC, 16)
        for tt in range(2):
            srcs = [XEV[2 * tt].ap(), XEV[2 * tt + 1].ap()]
            outs = [yev[seg, 128 * (2 * tt):128 * (2 * tt + 1), :],
                    yev[seg, 128 * (2 * tt + 1):128 * (2 * tt + 2), :]]
            ffn_tile(srcs, 4 * tt, outs, do_stats=False,
                     dbg=(seg == dbgseg and tt == 0))
            if stage == 4:
                return finish_dummy()
        for tt in range(2):
            if tt == 1:
                wt("scalar", dGO)
                for j in range(2):
                    xod_load[seg][2 + j] = opn(
                        "scalar", lambda e, seg=seg, j=j: e.dma_start(
                            out=XOD[j],
                            in_=xod[seg, 128 * (2 + j):128 * (3 + j), :]), dSC, 16)
            for j in range(2):
                wt("vector", dSC, xod_load[seg][2 * tt + j])
            srcs = [XOD[0], XOD[1]]
            outs = [yod[seg, 128 * (2 * tt):128 * (2 * tt + 1), :],
                    yod[seg, 128 * (2 * tt + 1):128 * (2 * tt + 2), :]]
            ffn_tile(srcs, 8 + 4 * tt, outs, do_stats=True)
        if stage == 5 and seg == 0:
            return finish_dummy()
        if seg + 1 < NSEG:
            wt("sync", dGO)
            for t in range(4):
                xev_load[seg + 1][t] = opn("sync", lambda e, s=seg + 1, t=t:
                                           e.dma_start(out=XEV[t].ap(),
                                                       in_=xev[s, 128 * t:128 * (t + 1), :]),
                                           dSY, 16)

    P.q["gpsimd"].append(lambda e: e.wait_ge(dGO.h, dGO.n))
    P.q["sync"].append(lambda e: e.wait_ge(dGO.h, dGO.n))

    rem = nc.sbuf_bytes_remaining
    with nc.Block() as blk:
        blk.sync(lambda e: [f(e) for f in P.q["sync"]])
        blk.tensor(lambda e: [f(e) for f in P.q["tensor"]])
        blk.vector(lambda e: [f(e) for f in P.q["vector"]])
        blk.scalar(lambda e: [f(e) for f in P.q["scalar"]])
        blk.gpsimd(lambda e: [f(e) for f in P.q["gpsimd"]])
    return nc, rem


_NC_CACHE = {}


def _get_nc():
    if "nc" not in _NC_CACHE:
        _NC_CACHE["nc"] = build()[0]
    return _NC_CACHE["nc"]


def kernel(L0_attn_x, L0_ffn_W_fc, L0_ffn_b_fc, L0_ffn_A_fc, L0_ffn_B_fc,
           L0_ffn_W_proj, L0_ffn_b_proj, L0_ffn_A_proj, L0_ffn_B_proj,
           L0_c, L0_wmod):
    x = np.ascontiguousarray(np.asarray(L0_attn_x, dtype=np.float32))
    wfcTh = np.ascontiguousarray(np.asarray(L0_ffn_W_fc, np.float32).T)
    bfc_ch = np.ascontiguousarray(np.asarray(L0_ffn_b_fc, np.float32).reshape(32, 128).T)
    afcTh = np.ascontiguousarray(np.asarray(L0_ffn_A_fc, np.float32).T)
    b2fcTh = np.ascontiguousarray(np.asarray(L0_ffn_B_fc, np.float32).T)
    wpjTh = np.ascontiguousarray(np.asarray(L0_ffn_W_proj, np.float32).T)
    bpj_ch = np.ascontiguousarray(np.asarray(L0_ffn_b_proj, np.float32).reshape(8, 128).T)
    apjTh = np.ascontiguousarray(np.asarray(L0_ffn_A_proj, np.float32).T)
    b2pjTh = np.ascontiguousarray(np.asarray(L0_ffn_B_proj, np.float32).T)
    cc = np.asarray(L0_c, np.float32)
    wmodh = np.ascontiguousarray(np.asarray(L0_wmod, np.float32))

    in_maps = []
    for core in range(8):
        b = core // 4
        s0 = (core % 4) * 2
        xs = x[b, s0 * SEG:(s0 + 2) * SEG].reshape(NSEG, SEG, D)
        in_maps.append({
            "xev": np.ascontiguousarray(xs[:, 0::2, :]),
            "xod": np.ascontiguousarray(xs[:, 1::2, :]),
            "c1": np.ascontiguousarray(cc[b:b + 1, :]),
            "wmod": wmodh, "wfcT": wfcTh, "bfc_c": bfc_ch, "afcT": afcTh,
            "b2fcT": b2fcTh, "wpjT": wpjTh, "bpj_c": bpj_ch, "apjT": apjTh,
            "b2pjT": b2pjTh,
        })
    nc = _get_nc()
    res = run_bass_kernel_spmd(nc, in_maps, core_ids=list(range(8)))
    out = np.empty((2, 8192, D), np.float32)
    for core in range(8):
        b = core // 4
        s0 = (core % 4) * 2
        r = res.results[core]
        o = out[b, s0 * SEG:(s0 + 2) * SEG].reshape(NSEG, SEG, D)
        o[:, 0::2, :] = r["yev"]
        o[:, 1::2, :] = r["yod"]
    return out
